# version 03
# baseline (speedup 1.0000x reference)
import contextlib
import math
import numpy as np
import concourse.bass as bass
import concourse.mybir as mybir
from concourse.bass_utils import run_bass_kernel_spmd

F32 = mybir.dt.float32
AF = mybir.ActivationFunctionType
ALU = mybir.AluOpType

NCORES = 8
SEQ = 4096
DM = 2048
NT = DM // 128
HALO = 256
OWN = 1024
T = HALO + OWN
CH = [(0, 512), (512, 512), (1024, 256)]
OCH = [(256, 512), (768, 512)]
LCH = [(0, 512), (512, 512)]
HEAD_DIM = 128
ROPE_THETA = 10000.0
LN_EPS = 1e-5
RMS_EPS = 1e-6
INTER0 = 8192
NI0 = INTER0 // 128
INTER1 = 5632
NI1 = INTER1 // 128
NGCH = T // 64
NHCH = HALO // 64
NTT = T // 128

ENGN = {"T": "tensor", "V": "vector", "S": "scalar", "DL": "sync", "DG": "gpsimd"}
NDMASEM = 8
SEMKEYS = (["T", "V", "S"]
           + [f"DL{i}" for i in range(NDMASEM)] + [f"DG{i}" for i in range(NDMASEM)])


class Prog:

    def __init__(self, nc, block, sems):
        self.nc = nc
        self.block = block
        self.sems = sems
        self.cnt = {k: 0 for k in SEMKEYS}
        self.pend = {k: [] for k in ENGN}
        self.lastw = {}
        self.readers = {}
        self.wm = {k: {} for k in ENGN}
        self.rr = {"DL": 0, "DG": 0}
        self.nops = 0

    def _edge(self, waits, eng, prod):
        peng, pcnt = prod
        if peng == eng and pcnt > self.cnt[eng]:
            return
        if self.wm[eng].get(peng, 0) >= pcnt:
            return
        if waits.get(peng, 0) < pcnt:
            waits[peng] = pcnt

    def op(self, eng, fn, reads=(), writes=(), inc=True):
        waits = {}
        rn = [a.tensor.name for a in reads]
        wn = [a.tensor.name for a in writes]
        for t in rn:
            pw = self.lastw.get(t)
            if pw is not None:
                self._edge(waits, eng, pw)
        for t in wn:
            pw = self.lastw.get(t)
            if pw is not None:
                self._edge(waits, eng, pw)
            for pe, pc in self.readers.get(t, {}).items():
                self._edge(waits, eng, (pe, pc))
        if eng in ("DL", "DG"):
            sk_own = f"{eng}{self.rr[eng] % NDMASEM}"
            self.rr[eng] += 1
            step = 16
            self._edge(waits, eng, (sk_own, self.cnt[sk_own]))
            self.cnt[sk_own] += 16
            comp = self.cnt[sk_own]
            skey = sk_own
        else:
            step = 1
            skey = eng
            if inc:
                self.cnt[eng] += 1
                comp = self.cnt[eng]
            else:
                comp = self.cnt[eng] + 1
        sems = self.sems

        def run(e, fn=fn, waits=waits, inc=inc, step=step, skey=skey):
            for sk, v in waits.items():
                e.wait_ge(sems[sk], v)
            inst = fn(e)
            if inc:
                inst.then_inc(sems[skey], step)

        self.pend[eng].append(run)
        self.nops += 1
        for sk, v in waits.items():
            if self.wm[eng].get(sk, 0) < v:
                self.wm[eng][sk] = v
        for t in rn:
            dd = self.readers.setdefault(t, {})
            if dd.get(skey, 0) < comp:
                dd[skey] = comp
        for t in wn:
            self.lastw[t] = (skey, comp)
            self.readers[t] = {}

    def flush(self):
        for k, fns in self.pend.items():
            if not fns:
                continue
            self.pend[k] = []
            getattr(self.block, ENGN[k])(lambda e, fns=fns: [f(e) for f in fns])

    def barrier(self):
        self.flush()
        sems, cnt = self.sems, dict(self.cnt)
        for k in ENGN:
            waits = [(sk, cnt[sk]) for sk in SEMKEYS if sk != k and cnt[sk] > 0]

            def run(e, waits=waits):
                for sk, v in waits:
                    e.wait_ge(sems[sk], v)
            self.pend[k].append(run)
        self.flush()
        for k in ENGN:
            for sk in SEMKEYS:
                self.wm[k][sk] = cnt[sk]
        self.lastw = {}
        self.readers = {}

    def dma(self, q, out, in_):
        self.op(q, lambda e: e.dma_start(out=out, in_=in_), reads=[in_], writes=[out])

    def mm(self, ps, lhsT, rhs, start, stop):
        self.op("T", lambda e: e.matmul(ps, lhsT=lhsT, rhs=rhs, start=start, stop=stop),
                reads=[lhsT, rhs], writes=[ps])

    def tr(self, ps, in_, ident):
        self.op("T", lambda e: e.transpose(ps, in_, ident), reads=[in_, ident], writes=[ps])

    def tt(self, out, in0, in1, alu):
        self.op("V", lambda e: e.tensor_tensor(out=out, in0=in0, in1=in1, op=alu),
                reads=[in0, in1], writes=[out])

    def ts(self, out, in0, s1, op0, s2=None, op1=ALU.bypass):
        r = [in0] + [x for x in (s1, s2) if isinstance(x, bass.AP)]
        self.op("V", lambda e: e.tensor_scalar(out=out, in0=in0, scalar1=s1, scalar2=s2,
                                               op0=op0, op1=op1), reads=r, writes=[out])

    def stt(self, out, in0, s, op0, in1, op1):
        r = [in0, in1] + ([s] if isinstance(s, bass.AP) else [])
        self.op("V", lambda e: e.scalar_tensor_tensor(out=out, in0=in0, scalar=s, in1=in1,
                                                      op0=op0, op1=op1), reads=r, writes=[out])

    def act(self, out, in_, func, bias=0.0, scale=1.0):
        r = [in_] + [x for x in (bias, scale) if isinstance(x, bass.AP)]
        self.op("S", lambda e: e.activation(out=out, in_=in_, func=func, bias=bias, scale=scale),
                reads=r, writes=[out])

    def scopy(self, out, in_):
        self.op("S", lambda e: e.copy(out=out, in_=in_), reads=[in_], writes=[out])

    def vcopy(self, out, in_):
        self.op("V", lambda e: e.tensor_copy(out=out, in_=in_), reads=[in_], writes=[out])

    def memset(self, ap, v):
        self.op("V", lambda e: e.memset(ap, v), reads=[], writes=[ap])

    def recip(self, out, in_):
        self.op("V", lambda e: e.reciprocal(out=out, in_=in_), reads=[in_], writes=[out])



def build_nc():
    nc = bass.Bass("TRN2", target_bir_lowering=False)
    d = {}

    def din(name, shape, dtype=F32):
        d[name] = nc.dram_tensor(name, list(shape), dtype, kind="ExternalInput").ap()

    def dout(name, shape, dtype=F32):
        d[name] = nc.dram_tensor(name, list(shape), dtype, kind="ExternalOutput").ap()

    def dtmp(name, shape, dtype=F32):
        d[name] = nc.dram_tensor(name, list(shape), dtype, kind="Internal").ap()

    din("xbT", (DM, SEQ))
    din("xoT", (DM, T))
    din("Wq0", (DM, DM))
    din("Wk0", (DM, 512))
    din("Wv0", (DM, 512))
    din("Wo0", (DM, DM))
    din("su_ug", (128, NT))
    din("su_dp", (128, NI0))
    din("sc_up", (128, NI0))
    din("sc_gate", (128, NI0))
    din("sc_dp", (128, NT))
    din("WupT", (DM, INTER0))
    din("WgT", (DM, INTER0))
    din("WdnT", (INTER0, DM))
    din("Wq1T", (DM, DM))
    din("Wk1T", (DM, DM))
    din("Wv1T", (DM, DM))
    din("Wo1T", (DM, DM))
    din("nw1", (128, NT))
    din("G1T", (DM, INTER1))
    din("U1T", (DM, INTER1))
    din("D1T", (INTER1, DM))
    din("g1s", (128, NI1))
    din("u1s", (128, NI1))
    din("d1s", (128, NT))
    din("ropeCq", (128, T))
    din("ropeSq", (128, T))
    din("ropeCk", (128, SEQ))
    din("ropeSk", (128, SEQ))
    din("s128c", (128, 1))
    din("Pswap", (128, 128))
    din("idn", (128, 128))
    din("idn64", (128, 64))
    din("H128", (128, 128))
    din("hmask", (128, 1))
    din("LT128", (128, 128))
    din("maskT", (128, 64))

    dout("outT", (DM, OWN))

    dtmp("x1_hbm", (NT, 128, T))
    dtmp("y_hbm", (NI0, 128, T))
    dtmp("x3_hbm", (NT, 128, T))
    dtmp("x4_hbm", (NT, 128, OWN))
    dtmp("qs_hbm", (NT, 128, OWN))
    dtmp("v_hbm", (NTT, 128, DM))
    dtmp("kd_hbm", (NTT, 128, DM))
    dtmp("A_hbm", (NTT, 128, DM))

    with contextlib.ExitStack() as semstack:
        sems = {k: semstack.enter_context(nc.semaphore("sem" + k)) for k in SEMKEYS}
        block = semstack.enter_context(nc.Block())
        p = Prog(nc, block, sems)

        PS = [nc.alloc_psum_tensor(f"PS{i}", [128, 512], F32).ap() for i in range(8)]

        ones_c = nc.alloc_sbuf_tensor("ones_c", [128, 1], F32).ap()
        ones_r = nc.alloc_sbuf_tensor("ones_r", [1, 128], F32).ap()
        idn_s = nc.alloc_sbuf_tensor("idn_s", [128, 128], F32).ap()
        idn64_s = nc.alloc_sbuf_tensor("idn64_s", [128, 64], F32).ap()
        psw_s = nc.alloc_sbuf_tensor("psw_s", [128, 128], F32).ap()
        h128_s = nc.alloc_sbuf_tensor("h128_s", [128, 128], F32).ap()
        epsln = nc.alloc_sbuf_tensor("epsln", [128, 1], F32).ap()
        epsrm = nc.alloc_sbuf_tensor("epsrm", [128, 1], F32).ap()
        rs2048 = nc.alloc_sbuf_tensor("rs2048", [128, 1], F32).ap()
        rs8192 = nc.alloc_sbuf_tensor("rs8192", [128, 1], F32).ap()
        p.memset(ones_c, 1.0)
        p.memset(ones_r, 1.0)
        p.memset(epsln, LN_EPS)
        p.memset(epsrm, RMS_EPS)
        for _ap, _n in ((rs2048, 2048.0), (rs8192, 8192.0)):
            p.memset(_ap, _n)
            p.act(_ap, _ap, AF.Sqrt)
            p.recip(_ap, _ap)
        p.dma("DL", idn_s, d["idn"])
        p.dma("DL", idn64_s, d["idn64"])
        p.dma("DL", psw_s, d["Pswap"])
        p.dma("DL", h128_s, d["H128"])

        _uid = [0]

        def alloc(stack, name, shape, dtype=F32):
            _uid[0] += 1
            return stack.enter_context(
                nc.sbuf_tensor(f"{name}_u{_uid[0]}", list(shape), dtype)).ap()

        def rope(dst, W, ps_tr, Ct, St, scr):
            o = 0
            while o < W:
                w = min(512, W - o)
                p.mm(ps_tr[:, 0:w], psw_s, dst[:, o:o + w], start=True, stop=True)
                p.tt(scr[:, 0:w], ps_tr[:, 0:w], St[:, o:o + w], ALU.mult)
                p.tt(dst[:, o:o + w], dst[:, o:o + w], Ct[:, o:o + w], ALU.mult)
                p.tt(dst[:, o:o + w], dst[:, o:o + w], scr[:, 0:w], ALU.add)
                o += w

        def layernorm_tiles(xt, stack, chunks, store=None):
            ps_a, ps_b, ps_c = PS[0], PS[1], PS[2]
            sq = alloc(stack, "ln_sq", (128, 512))
            mrow = alloc(stack, "ln_m", (1, 512))
            srow = alloc(stack, "ln_s", (1, 512))
            vrow = alloc(stack, "ln_v", (1, 512))
            rrow = alloc(stack, "ln_r", (1, 512))
            mB = alloc(stack, "ln_mB", (128, 512))
            rB = alloc(stack, "ln_rB", (128, 512))
            for c0, cw in chunks:
                for j in range(NT):
                    xs = xt[j][:, c0:c0 + cw]
                    p.mm(ps_a[0:1, 0:cw], ones_c, xs, start=(j == 0), stop=(j == NT - 1))
                    p.act(sq[:, 0:cw], xs, AF.Square)
                    p.mm(ps_b[0:1, 0:cw], ones_c, sq[:, 0:cw], start=(j == 0), stop=(j == NT - 1))
                p.ts(mrow[:, 0:cw], ps_a[0:1, 0:cw], 1.0 / DM, ALU.mult)
                p.ts(srow[:, 0:cw], ps_b[0:1, 0:cw], 1.0 / DM, ALU.mult)
                p.tt(vrow[:, 0:cw], mrow[:, 0:cw], mrow[:, 0:cw], ALU.mult)
                p.tt(vrow[:, 0:cw], srow[:, 0:cw], vrow[:, 0:cw], ALU.subtract)
                p.act(vrow[:, 0:cw], vrow[:, 0:cw], AF.Sqrt, bias=epsln[0:1, :])
                p.recip(rrow[:, 0:cw], vrow[:, 0:cw])
                p.mm(ps_c[:, 0:cw], ones_r, mrow[0:1, 0:cw], start=True, stop=True)
                p.scopy(mB[:, 0:cw], ps_c[:, 0:cw])
                p.mm(ps_c[:, 0:cw], ones_r, rrow[0:1, 0:cw], start=True, stop=True)
                p.scopy(rB[:, 0:cw], ps_c[:, 0:cw])
                for j in range(NT):
                    xs = xt[j][:, c0:c0 + cw]
                    p.tt(xs, xs, mB[:, 0:cw], ALU.subtract)
                    p.tt(xs, xs, rB[:, 0:cw], ALU.mult)
                    if store is not None:
                        p.dma("DG", store(j, c0, cw), xs)

        with contextlib.ExitStack() as stP1:
            oac = [alloc(stP1, f"oac{i}", (128, T)) for i in range(NT)]
            with contextlib.ExitStack() as st1a:
                Kg = alloc(st1a, "Kg", (128, SEQ))
                Vg = alloc(st1a, "Vg", (128, SEQ))
                qf = alloc(st1a, "qf", (128, T))
                Cq = alloc(st1a, "Cq", (128, T))
                Sq = alloc(st1a, "Sq", (128, T))
                ckb = alloc(st1a, "ckb", (128, 512))
                skb = alloc(st1a, "skb", (128, 512))
                s128c = alloc(st1a, "s128c", (128, 1))
                wk_t = [alloc(st1a, f"wk{m}", (128, 128)) for m in range(NT)]
                wv_t = [alloc(st1a, f"wv{m}", (128, 128)) for m in range(NT)]
                wq_t = [alloc(st1a, f"wqt{m}", (128, 128)) for m in range(NT)]
                xb_b = [alloc(st1a, f"xb{i}", (128, 512)) for i in range(4)]
                vts = alloc(st1a, "vts", (128, 512))
                rsc = alloc(st1a, "rsc", (128, 512))
                pb = [alloc(st1a, f"pb{i}", (128, 512)) for i in range(2)]
                Dd = alloc(st1a, "Dd", (128, 512))
                bcs = alloc(st1a, "bcs", (128, 512))
                rr1 = alloc(st1a, "rr1", (1, 512))

                p.dma("DL", Cq, d["ropeCq"])
                p.dma("DL", Sq, d["ropeSq"])
                p.dma("DL", s128c, d["s128c"])

                ps_k, ps_v, ps_tr, ps_q, ps_s0, ps_s1, ps_o, ps_sm = PS

                for g in range(4):
                    for m in range(NT):
                        p.dma("DL", wk_t[m], d["Wk0"][128 * m:128 * m + 128, 128 * g:128 * g + 128])
                        p.dma("DL", wv_t[m], d["Wv0"][128 * m:128 * m + 128, 128 * g:128 * g + 128])
                    for c in range(SEQ // 512):
                        c0 = 512 * c
                        for m in range(NT):
                            xb = xb_b[m % 4]
                            p.dma("DL", xb, d["xbT"][128 * m:128 * m + 128, c0:c0 + 512])
                            p.mm(ps_k, wk_t[m], xb, start=(m == 0), stop=(m == NT - 1))
                            p.mm(ps_v, wv_t[m], xb, start=(m == 0), stop=(m == NT - 1))
                        kf = Kg[:, c0:c0 + 512]
                        p.scopy(kf, ps_k)
                        p.dma("DL", ckb, d["ropeCk"][:, c0:c0 + 512])
                        p.dma("DL", skb, d["ropeSk"][:, c0:c0 + 512])
                        rope(kf, 512, ps_tr, ckb, skb, rsc)
                        p.scopy(vts, ps_v)
                        for kk in range(4):
                            tt_i = 4 * c + kk
                            p.tr(ps_tr[:, 0:128], vts[:, 128 * kk:128 * kk + 128], idn_s)
                            p.scopy(Vg[:, 128 * tt_i:128 * tt_i + 128], ps_tr[:, 0:128])
                    for h in range(4):
                        gh = 4 * g + h
                        for m in range(NT):
                            p.dma("DL", wq_t[m], d["Wq0"][128 * m:128 * m + 128, 128 * gh:128 * gh + 128])
                        for c0, cw in CH:
                            for m in range(NT):
                                xb = xb_b[m % 4]
                                p.dma("DL", xb[:, 0:cw], d["xoT"][128 * m:128 * m + 128, c0:c0 + cw])
                                p.mm(ps_q[:, 0:cw], wq_t[m], xb[:, 0:cw],
                                     start=(m == 0), stop=(m == NT - 1))
                            p.scopy(qf[:, c0:c0 + cw], ps_q[:, 0:cw])
                        rope(qf, T, ps_tr, Cq, Sq, rsc)
                        for qb0, qbw in CH:
                            qv = qf[:, qb0:qb0 + qbw]
                            sb = [ps_s0, ps_s1]
                            nkt = SEQ // 128
                            p.mm(sb[0][:, 0:qbw], Kg[:, 0:128], qv, start=True, stop=True)
                            for kt in range(nkt):
                                pbuf = pb[kt % 2][:, 0:qbw]
                                p.act(pbuf, sb[kt % 2][:, 0:qbw], AF.Exp, scale=s128c)
                                if kt + 1 < nkt:
                                    p.mm(sb[(kt + 1) % 2][:, 0:qbw],
                                         Kg[:, 128 * (kt + 1):128 * (kt + 2)], qv,
                                         start=True, stop=True)
                                if kt == 0:
                                    p.vcopy(Dd[:, 0:qbw], pbuf)
                                else:
                                    p.tt(Dd[:, 0:qbw], Dd[:, 0:qbw], pbuf, ALU.add)
                                p.mm(ps_o[:, 0:qbw], Vg[:, 128 * kt:128 * kt + 128], pbuf,
                                     start=(kt == 0), stop=(kt == nkt - 1))
                            p.mm(ps_sm[0:1, 0:qbw], ones_c, Dd[:, 0:qbw], start=True, stop=True)
                            p.recip(rr1[:, 0:qbw], ps_sm[0:1, 0:qbw])
                            p.mm(ps_sm[:, 0:qbw], ones_r, rr1[0:1, 0:qbw], start=True, stop=True)
                            p.scopy(bcs[:, 0:qbw], ps_sm[:, 0:qbw])
                            p.tt(oac[gh][:, qb0:qb0 + qbw], ps_o[:, 0:qbw], bcs[:, 0:qbw], ALU.mult)
                p.barrier()

            with contextlib.ExitStack() as st1b:
                x1t = [alloc(st1b, f"x1t{j}", (128, T)) for j in range(NT)]
                wo_t = [alloc(st1b, f"wo{i}", (128, 128)) for i in range(NT)]
                xr_b = [alloc(st1b, f"xr{i}", (128, 512)) for i in range(2)]
                ps_x = PS[3]
                for j in range(NT):
                    for i in range(NT):
                        p.dma("DL", wo_t[i], d["Wo0"][128 * i:128 * i + 128, 128 * j:128 * j + 128])
                    for ci, (c0, cw) in enumerate(CH):
                        for i in range(NT):
                            p.mm(ps_x[:, 0:cw], wo_t[i], oac[i][:, c0:c0 + cw],
                                 start=(i == 0), stop=(i == NT - 1))
                        xr = xr_b[ci % 2]
                        p.dma("DL", xr[:, 0:cw], d["xoT"][128 * j:128 * j + 128, c0:c0 + cw])
                        p.tt(x1t[j][:, c0:c0 + cw], ps_x[:, 0:cw], xr[:, 0:cw], ALU.add)
                layernorm_tiles(x1t, st1b, CH,
                                store=lambda j, c0, cw: d["x1_hbm"][j, :, c0:c0 + cw])
                p.barrier()

        with contextlib.ExitStack() as stP2:
            with contextlib.ExitStack() as st2a:
                zA = [alloc(st2a, f"zA{j}", (128, T)) for j in range(NT)]
                with contextlib.ExitStack() as st2z:
                    zB = [alloc(st2z, f"zB{j}", (128, T)) for j in range(NT)]
                    Hs = [alloc(st2z, f"Hsu{j}", (128, 128)) for j in range(NT)]
                    suc = alloc(st2z, "suc", (128, NT))
                    xb_b = [alloc(st2z, f"fxb{i}", (128, 512)) for i in range(2)]
                    p.dma("DL", suc, d["su_ug"])
                    for j in range(NT):
                        p.ts(Hs[j], h128_s, suc[:, j:j + 1], ALU.mult)
                    for j in range(NT):
                        for ci, (c0, cw) in enumerate(CH):
                            xb = xb_b[ci % 2]
                            p.dma("DL", xb[:, 0:cw], d["x1_hbm"][j, :, c0:c0 + cw])
                            p.mm(PS[0][:, 0:cw], Hs[j], xb[:, 0:cw], start=True, stop=True)
                            p.scopy(zA[j][:, c0:c0 + cw], PS[0][:, 0:cw])
                    cur, nxt = zA, zB
                    for hd in (1, 2, 4, 8):
                        for j in range(NT):
                            a, b2 = cur[j & ~hd], cur[j | hd]
                            p.tt(nxt[j], a, b2, ALU.add if (j & hd) == 0 else ALU.subtract)
                        cur, nxt = nxt, cur
                    assert cur is zA
                    p.barrier()

                wu_t = [alloc(st2a, f"wu{m}", (128, 128)) for m in range(NT)]
                wg_t = [alloc(st2a, f"wg{m}", (128, 128)) for m in range(NT)]
                silb = alloc(st2a, "silb", (128, 512))
                upb = alloc(st2a, "upb", (128, 512))
                yb = alloc(st2a, "yb", (128, 512))
                scU = alloc(st2a, "scU", (128, NI0))
                scG = alloc(st2a, "scG", (128, NI0))
                sud = alloc(st2a, "sud", (128, NI0))
                p.dma("DL", scU, d["sc_up"])
                p.dma("DL", scG, d["sc_gate"])
                p.dma("DL", sud, d["su_dp"])
                p.ts(scU, scU, rs2048, ALU.mult)
                p.ts(scG, scG, rs2048, ALU.mult)
                ps_u, ps_g = PS[1], PS[2]
                for i in range(NI0):
                    for m in range(NT):
                        p.dma("DL", wu_t[m], d["WupT"][128 * m:128 * m + 128, 128 * i:128 * i + 128])
                        p.dma("DL", wg_t[m], d["WgT"][128 * m:128 * m + 128, 128 * i:128 * i + 128])
                    for c0, cw in CH:
                        for m in range(NT):
                            p.mm(ps_u[:, 0:cw], wu_t[m], zA[m][:, c0:c0 + cw],
                                 start=(m == 0), stop=(m == NT - 1))
                            p.mm(ps_g[:, 0:cw], wg_t[m], zA[m][:, c0:c0 + cw],
                                 start=(m == 0), stop=(m == NT - 1))
                        p.act(silb[:, 0:cw], ps_g[:, 0:cw], AF.Silu, scale=scG[:, i:i + 1])
                        p.act(upb[:, 0:cw], ps_u[:, 0:cw], AF.Copy, scale=scU[:, i:i + 1])
                        p.stt(yb[:, 0:cw], silb[:, 0:cw], sud[:, i:i + 1], ALU.mult,
                              upb[:, 0:cw], ALU.mult)
                        p.dma("DG", d["y_hbm"][i, :, c0:c0 + cw], yb[:, 0:cw])
                p.barrier()

            x2t = [alloc(stP2, f"x2t{j}", (128, T)) for j in range(NT)]
            with contextlib.ExitStack() as st2c:
                yA = [alloc(st2c, f"yA{i}", (128, 128)) for i in range(NI0)]
                yB = [alloc(st2c, f"yB{i}", (128, 128)) for i in range(NI0)]
                wd_t = [alloc(st2c, f"wd{i}", (128, 128)) for i in range(16)]
                xr_b = [alloc(st2c, f"cxr{i}", (128, 128)) for i in range(2)]
                scD = alloc(st2c, "scD", (128, NT))
                p.dma("DL", scD, d["sc_dp"])
                p.ts(scD, scD, rs8192, ALU.mult)
                for c5 in range(T // 128):
                    c0 = 128 * c5
                    for i in range(NI0):
                        p.dma("DL", yA[i], d["y_hbm"][i, :, c0:c0 + 128])
                        p.mm(PS[0][:, 0:128], h128_s, yA[i], start=True, stop=True)
                        p.scopy(yA[i], PS[0][:, 0:128])
                    cur, nxt = yA, yB
                    for hd in (1, 2, 4, 8, 16, 32):
                        for i in range(NI0):
                            a, b2 = cur[i & ~hd], cur[i | hd]
                            p.tt(nxt[i], a, b2, ALU.add if (i & hd) == 0 else ALU.subtract)
                        cur, nxt = nxt, cur
                    assert cur is yA
                    for jg in range(2):
                        for i in range(NI0):
                            for j8 in range(8):
                                j = 8 * jg + j8
                                wd = wd_t[8 * (i % 2) + j8]
                                p.dma("DL", wd,
                                      d["WdnT"][128 * i:128 * i + 128, 128 * j:128 * j + 128])
                                p.mm(PS[j8][:, 0:128], wd, yA[i],
                                     start=(i == 0), stop=(i == NI0 - 1))
                        for j8 in range(8):
                            j = 8 * jg + j8
                            xr = xr_b[j8 % 2]
                            p.dma("DL", xr, d["x1_hbm"][j, :, c0:c0 + 128])
                            p.stt(x2t[j][:, c0:c0 + 128], PS[j8][:, 0:128], scD[:, j:j + 1],
                                  ALU.mult, xr, ALU.add)
                p.barrier()

            with contextlib.ExitStack() as st2d:
                layernorm_tiles(x2t, st2d, CH,
                                store=lambda j, c0, cw: d["x3_hbm"][j, :, c0:c0 + cw])
                p.barrier()

            x3t = x2t
            with contextlib.ExitStack() as st3a:
                wq1 = [alloc(st3a, f"wq1{m}", (128, 128)) for m in range(NT)]
                qsb = alloc(st3a, "qsb", (128, 512))
                ps_q = PS[3]
                for et in range(NT):
                    for m in range(NT):
                        p.dma("DL", wq1[m], d["Wq1T"][128 * m:128 * m + 128, 128 * et:128 * et + 128])
                    for c0, cw in OCH:
                        for m in range(NT):
                            p.mm(ps_q[:, 0:cw], wq1[m], x3t[m][:, c0:c0 + cw],
                                 start=(m == 0), stop=(m == NT - 1))
                        p.act(qsb[:, 0:cw], ps_q[:, 0:cw], AF.Silu)
                        p.dma("DG", d["qs_hbm"][et, :, c0 - HALO:c0 - HALO + cw], qsb[:, 0:cw])

                wkc = [alloc(st3a, f"wkc{i}", (128, 512)) for i in range(2)]
                wvc = [alloc(st3a, f"wvc{i}", (128, 512)) for i in range(2)]
                fb = alloc(st3a, "fb", (128, 512))
                kb = alloc(st3a, "kb", (128, 512))
                eb = alloc(st3a, "eb", (128, 512))
                Ab = alloc(st3a, "Ab", (128, 512))
                vb = alloc(st3a, "vb", (128, 512))
                lt128 = alloc(st3a, "lt128", (128, 128))
                hmk = alloc(st3a, "hmk", (128, 1))
                p.dma("DL", lt128, d["LT128"])
                p.dma("DL", hmk, d["hmask"])
                ps_f, ps_v, ps_A = PS[4], PS[5], PS[6]
                for tt_i in range(NTT):
                    for ec in range(4):
                        e0 = 512 * ec
                        for m in range(NT):
                            w1, w2 = wkc[m % 2], wvc[m % 2]
                            p.dma("DL", w1, d["Wk1T"][128 * m:128 * m + 128, e0:e0 + 512])
                            p.dma("DL", w2, d["Wv1T"][128 * m:128 * m + 128, e0:e0 + 512])
                            xsl = x3t[m][:, 128 * tt_i:128 * tt_i + 128]
                            p.mm(ps_f, xsl, w1, start=(m == 0), stop=(m == NT - 1))
                            p.mm(ps_v, xsl, w2, start=(m == 0), stop=(m == NT - 1))
                        p.act(fb, ps_f, AF.Sigmoid)
                        p.act(kb, fb, AF.Sigmoid, scale=-1.0)
                        p.act(fb, fb, AF.Sigmoid)
                        p.act(fb, fb, AF.Ln)
                        p.mm(ps_A, lt128, fb, start=True, stop=True)
                        p.scopy(Ab, ps_A)
                        p.act(eb, ps_A, AF.Exp, scale=-1.0)
                        p.tt(kb, kb, eb, ALU.mult)
                        if tt_i < 2:
                            p.act(vb, ps_v, AF.Copy, scale=hmk)
                        else:
                            p.scopy(vb, ps_v)
                        p.dma("DG", d["A_hbm"][tt_i, :, e0:e0 + 512], Ab)
                        p.dma("DG", d["kd_hbm"][tt_i, :, e0:e0 + 512], kb)
                        p.dma("DG", d["v_hbm"][tt_i, :, e0:e0 + 512], vb)
                p.barrier()

        with contextlib.ExitStack() as stP3:
            ot = [alloc(stP3, f"ot{i}", (128, OWN)) for i in range(NT)]
            with contextlib.ExitStack() as st3b:
                qsg = [alloc(st3b, f"qsg{h}", (128, OWN)) for h in range(4)]
                vg = [alloc(st3b, f"vg{c}", (128, 512)) for c in range(NTT)]
                kdg = [alloc(st3b, f"kdg{c}", (128, 512)) for c in range(NTT)]
                Ag = [alloc(st3b, f"Agc{c}", (128, 512)) for c in range(NTT)]
                kdT = [alloc(st3b, f"kdT{h}", (128, OWN)) for h in range(4)]
                AfT = alloc(st3b, "AfT", (128, T))
                dec = [alloc(st3b, f"dec{h}", (128, NGCH)) for h in range(4)]
                Ssb = [alloc(st3b, f"Ssb{h}", (128, 128)) for h in range(4)]
                psb = alloc(st3b, "psb", (128, 64))
                mskT = alloc(st3b, "mskT", (128, 64))
                p.dma("DL", mskT, d["maskT"])
                ps_tr, ps_sc, ps_go, ps_sd = PS[0], PS[1], PS[2], PS[3]
                for g in range(4):
                    e0 = 512 * g
                    for h in range(4):
                        p.dma("DL", qsg[h], d["qs_hbm"][4 * g + h])
                    for c in range(NTT):
                        p.dma("DL", vg[c], d["v_hbm"][c, :, e0:e0 + 512])
                        p.dma("DL", kdg[c], d["kd_hbm"][c, :, e0:e0 + 512])
                        p.dma("DL", Ag[c], d["A_hbm"][c, :, e0:e0 + 512])
                    for h in range(4):
                        he = 128 * h
                        for c in range(NGCH):
                            r0 = 64 * (c % 2)
                            tle = Ag[c // 2][r0:r0 + 64, he:he + 128]
                            p.tr(ps_tr[:, 0:64], tle, idn64_s[r0:r0 + 64, :])
                            p.scopy(AfT[:, 64 * c:64 * c + 64], ps_tr[:, 0:64])
                            if c >= NHCH:
                                oc = c - NHCH
                                tle = kdg[c // 2][r0:r0 + 64, he:he + 128]
                                p.tr(ps_tr[:, 64:128], tle, idn64_s[r0:r0 + 64, :])
                                p.scopy(kdT[h][:, 64 * oc:64 * oc + 64], ps_tr[:, 64:128])
                        p.act(AfT, AfT, AF.Exp)
                        p.ts(dec[h], AfT[:, 63::64], 1.0, ALU.mult)
                        p.tt(qsg[h], qsg[h], AfT[:, HALO:], ALU.mult)
                    for c in range(NGCH):
                        r0 = 64 * (c % 2)
                        for h in range(4):
                            he = 128 * h
                            gh = 4 * g + h
                            if c >= NHCH:
                                oc = c - NHCH
                                t0 = 64 * oc
                                p.mm(ps_sc[r0:r0 + 64, 0:64], kdT[h][:, t0:t0 + 64],
                                     qsg[h][:, t0:t0 + 64], start=True, stop=True)
                                p.tt(psb[r0:r0 + 64, :], ps_sc[r0:r0 + 64, 0:64],
                                     mskT[r0:r0 + 64, :], ALU.mult)
                                p.mm(ps_go[:, 0:64], vg[c // 2][r0:r0 + 64, he:he + 128],
                                     psb[r0:r0 + 64, :], start=True, stop=False)
                                p.mm(ps_go[:, 0:64], Ssb[h], qsg[h][:, t0:t0 + 64],
                                     start=False, stop=True)
                                p.scopy(ot[gh][:, t0:t0 + 64], ps_go[:, 0:64])
                            p.mm(ps_sd[:, 0:128], kdg[c // 2][r0:r0 + 64, he:he + 128],
                                 vg[c // 2][r0:r0 + 64, he:he + 128], start=True, stop=True)
                            if c == 0:
                                p.ts(Ssb[h], ps_sd[:, 0:128], dec[h][:, 0:1], ALU.mult)
                            else:
                                p.tt(Ssb[h], Ssb[h], ps_sd[:, 0:128], ALU.add)
                                p.ts(Ssb[h], Ssb[h], dec[h][:, c:c + 1], ALU.mult)
                p.barrier()

            with contextlib.ExitStack() as st3c:
                x4t = [alloc(st3c, f"x4t{j}", (128, OWN)) for j in range(NT)]
                nwc = alloc(st3c, "nwc", (128, NT))
                sq = alloc(st3c, "rm_sq", (128, 512))
                srow = alloc(st3c, "rm_s", (1, 512))
                rrow = alloc(st3c, "rm_r", (1, 512))
                rB = alloc(st3c, "rm_rB", (128, 512))
                p.dma("DL", nwc, d["nw1"])
                for c0, cw in LCH:
                    for j in range(NT):
                        p.act(sq[:, 0:cw], ot[j][:, c0:c0 + cw], AF.Square)
                        p.mm(PS[0][0:1, 0:cw], ones_c, sq[:, 0:cw],
                             start=(j == 0), stop=(j == NT - 1))
                    p.ts(srow[:, 0:cw], PS[0][0:1, 0:cw], 1.0 / DM, ALU.mult)
                    p.act(srow[:, 0:cw], srow[:, 0:cw], AF.Sqrt, bias=epsrm[0:1, :])
                    p.recip(rrow[:, 0:cw], srow[:, 0:cw])
                    p.mm(PS[1][:, 0:cw], ones_r, rrow[0:1, 0:cw], start=True, stop=True)
                    p.scopy(rB[:, 0:cw], PS[1][:, 0:cw])
                    for j in range(NT):
                        p.stt(ot[j][:, c0:c0 + cw], ot[j][:, c0:c0 + cw], nwc[:, j:j + 1],
                              ALU.mult, rB[:, 0:cw], ALU.mult)
                wo1 = [alloc(st3c, f"wo1{i}", (128, 128)) for i in range(NT)]
                xr_b = [alloc(st3c, f"xr3{i}", (128, 512)) for i in range(2)]
                for j in range(NT):
                    for i in range(NT):
                        p.dma("DL", wo1[i], d["Wo1T"][128 * i:128 * i + 128, 128 * j:128 * j + 128])
                    for ci, (c0, cw) in enumerate(LCH):
                        for i in range(NT):
                            p.mm(PS[2][:, 0:cw], wo1[i], ot[i][:, c0:c0 + cw],
                                 start=(i == 0), stop=(i == NT - 1))
                        xr = xr_b[ci % 2]
                        p.dma("DL", xr[:, 0:cw], d["x3_hbm"][j, :, HALO + c0:HALO + c0 + cw])
                        p.tt(x4t[j][:, c0:c0 + cw], PS[2][:, 0:cw], xr[:, 0:cw], ALU.add)
                layernorm_tiles(x4t, st3c, LCH,
                                store=lambda j, c0, cw: d["x4_hbm"][j, :, c0:c0 + cw])
                p.barrier()

        with contextlib.ExitStack() as stP4:
            x5t = [alloc(stP4, f"x5t{j}", (128, OWN)) for j in range(NT)]
            with contextlib.ExitStack() as st4a:
                xc = [alloc(st4a, f"xc{m}", (128, 256)) for m in range(NT)]
                y1 = [alloc(st4a, f"y1{i}", (128, 256)) for i in range(NI1)]
                wgt = [alloc(st4a, f"wgt{m}", (128, 128)) for m in range(NT)]
                wut = [alloc(st4a, f"wut{m}", (128, 128)) for m in range(NT)]
                wdt = [alloc(st4a, f"wdt{i}", (128, 128)) for i in range(16)]
                silb = alloc(st4a, "sil4", (128, 256))
                upb = alloc(st4a, "up4", (128, 256))
                g1c = alloc(st4a, "g1c", (128, NI1))
                u1c = alloc(st4a, "u1c", (128, NI1))
                d1c = alloc(st4a, "d1c", (128, NT))
                xr_b = [alloc(st4a, f"xr4{i}", (128, 256)) for i in range(2)]
                p.dma("DL", g1c, d["g1s"])
                p.dma("DL", u1c, d["u1s"])
                p.dma("DL", d1c, d["d1s"])
                ps_g, ps_u = PS[6], PS[7]
                for c4 in range(OWN // 256):
                    c0 = 256 * c4
                    for m in range(NT):
                        p.dma("DL", xc[m], d["x4_hbm"][m, :, c0:c0 + 256])
                    for i in range(NI1):
                        for m in range(NT):
                            p.dma("DL", wgt[m], d["G1T"][128 * m:128 * m + 128, 128 * i:128 * i + 128])
                            p.dma("DL", wut[m], d["U1T"][128 * m:128 * m + 128, 128 * i:128 * i + 128])
                            p.mm(ps_g[:, 0:256], wgt[m], xc[m], start=(m == 0), stop=(m == NT - 1))
                            p.mm(ps_u[:, 0:256], wut[m], xc[m], start=(m == 0), stop=(m == NT - 1))
                        p.act(silb, ps_g[:, 0:256], AF.Silu, scale=g1c[:, i:i + 1])
                        p.act(upb, ps_u[:, 0:256], AF.Copy, scale=u1c[:, i:i + 1])
                        p.tt(y1[i], silb, upb, ALU.mult)
                    for jg in range(2):
                        for i in range(NI1):
                            for j8 in range(8):
                                j = 8 * jg + j8
                                wd = wdt[8 * (i % 2) + j8]
                                p.dma("DL", wd, d["D1T"][128 * i:128 * i + 128, 128 * j:128 * j + 128])
                                p.mm(PS[j8][:, 0:256], wd, y1[i],
                                     start=(i == 0), stop=(i == NI1 - 1))
                        for j8 in range(8):
                            j = 8 * jg + j8
                            xr = xr_b[j8 % 2]
                            p.dma("DL", xr, d["x4_hbm"][j, :, c0:c0 + 256])
                            p.stt(x5t[j][:, c0:c0 + 256], PS[j8][:, 0:256], d1c[:, j:j + 1],
                                  ALU.mult, xr, ALU.add)
                p.barrier()
            with contextlib.ExitStack() as st4b:
                layernorm_tiles(x5t, st4b, LCH,
                                store=lambda j, c0, cw: d["outT"][128 * j:128 * j + 128, c0:c0 + cw])
                p.barrier()

        p.flush()
    return nc, d



def _rope_trig(positions):
    pos = np.asarray(positions)
    try:
        import jax
        import jax.numpy as jnp
        with jax.default_device(jax.devices("cpu")[0]):
            inv = 1.0 / ROPE_THETA ** (jnp.arange(0, 64, 2, dtype=jnp.float32) / 64)
            fr = pos.astype(np.float32)[None, :] * inv[:, None]
            cos = np.asarray(jnp.cos(fr), np.float32)
            sin = np.asarray(jnp.sin(fr), np.float32)
            return cos, sin
    except Exception:
        e = np.arange(0, 64, 2, dtype=np.float32) / np.float32(64)
        inv = (np.float32(1.0) / (np.float32(ROPE_THETA) ** e)).astype(np.float32)
        fr = (pos.astype(np.float32)[None, :] * inv[:, None]).astype(np.float32)
        return np.cos(fr).astype(np.float32), np.sin(fr).astype(np.float32)


def _rope_tables(positions):
    n = len(positions)
    cos, sin = _rope_trig(positions)
    C = np.ones((128, n), dtype=np.float32)
    S = np.zeros((128, n), dtype=np.float32)
    C[0:64:2] = cos
    C[1:64:2] = cos
    S[0:64:2] = -sin
    S[1:64:2] = sin
    return C, S


def _attn_scale():
    try:
        import jax
        import jax.numpy as jnp
        with jax.default_device(jax.devices("cpu")[0]):
            return float(1.0 / jnp.sqrt(jnp.float32(HEAD_DIM)))
    except Exception:
        return float(np.float32(1.0) / np.float32(np.sqrt(np.float32(HEAD_DIM))))


def _hadamard(n):
    H = np.array([[1.0]], dtype=np.float32)
    while H.shape[0] < n:
        H = np.block([[H, H], [H, -H]])
    return np.ascontiguousarray(H.astype(np.float32))


def _cols(v, rows=128):
    return np.ascontiguousarray(np.asarray(v, np.float32).reshape(-1, rows).T)


def prepare(inputs):
    ins = {k: np.asarray(v, dtype=np.float32) for k, v in inputs.items()}
    x = ins["L0_attn_hidden_states"]

    Pswap = np.zeros((128, 128), dtype=np.float32)
    for j in range(32):
        Pswap[2 * j, 2 * j + 1] = 1.0
        Pswap[2 * j + 1, 2 * j] = 1.0
    idn = np.eye(128, dtype=np.float32)
    idn64 = np.zeros((128, 64), dtype=np.float32)
    idn64[0:64] = np.eye(64, dtype=np.float32)
    idn64[64:128] = np.eye(64, dtype=np.float32)
    H128 = _hadamard(128)
    Ck, Sk = _rope_tables(np.arange(SEQ))
    tri = np.triu(np.ones((64, 64), dtype=np.float32))
    LT128 = np.zeros((128, 128), dtype=np.float32)
    LT128[0:64, 0:64] = tri
    LT128[64:128, 64:128] = tri
    maskT = np.concatenate([tri, tri], axis=0).astype(np.float32)

    base = {
        "Wq0": ins["L0_attn_Wq"], "Wk0": ins["L0_attn_Wk"],
        "Wv0": ins["L0_attn_Wv"], "Wo0": ins["L0_attn_Wo"],
        "su_ug": _cols(ins["L0_ffn_SU_ug"]), "su_dp": _cols(ins["L0_ffn_SU_dp"]),
        "sc_up": _cols(ins["L0_ffn_Wscale_ug"][:INTER0]),
        "sc_gate": _cols(ins["L0_ffn_Wscale_ug"][INTER0:]),
        "sc_dp": _cols(ins["L0_ffn_Wscale_dp"]),
        "WupT": np.ascontiguousarray(ins["L0_ffn_W_up"].T),
        "WgT": np.ascontiguousarray(ins["L0_ffn_W_gate"].T),
        "WdnT": np.ascontiguousarray(ins["L0_ffn_W_down"].T),
        "Wq1T": np.ascontiguousarray(ins["L1_attn_Wq"].T),
        "Wk1T": np.ascontiguousarray(ins["L1_attn_Wk"].T),
        "Wv1T": np.ascontiguousarray(ins["L1_attn_Wv"].T),
        "Wo1T": np.ascontiguousarray(ins["L1_attn_Wo"].T),
        "nw1": _cols(ins["L1_attn_norm_w"]),
        "G1T": np.ascontiguousarray(ins["L1_ffn_gate_w"].T),
        "U1T": np.ascontiguousarray(ins["L1_ffn_up_w"].T),
        "D1T": np.ascontiguousarray(ins["L1_ffn_down_w"].T),
        "g1s": _cols(ins["L1_ffn_gate_s"]), "u1s": _cols(ins["L1_ffn_up_s"]),
        "d1s": _cols(ins["L1_ffn_down_s"]),
        "ropeCk": Ck, "ropeSk": Sk,
        "s128c": np.full((128, 1), _attn_scale(), dtype=np.float32),
        "Pswap": Pswap, "idn": idn, "idn64": idn64, "H128": H128,
        "LT128": LT128, "maskT": maskT,
    }

    in_maps = []
    for c in range(NCORES):
        b, s = c // 4, c % 4
        xbT = np.ascontiguousarray(x[b].T)
        lo = 1024 * s - HALO
        if s == 0:
            halo = np.zeros((DM, HALO), dtype=np.float32)
            pos_h = np.zeros(HALO, dtype=np.int64)
        else:
            halo = xbT[:, lo:lo + HALO]
            pos_h = np.arange(lo, lo + HALO)
        own = xbT[:, 1024 * s:1024 * s + OWN]
        xoT = np.ascontiguousarray(np.concatenate([halo, own], axis=1))
        pos = np.concatenate([pos_h, np.arange(1024 * s, 1024 * s + OWN)])
        Cq, Sq = _rope_tables(pos)
        m = dict(base)
        m["xbT"] = xbT
        m["xoT"] = xoT
        m["ropeCq"] = Cq
        m["ropeSq"] = Sq
        m["hmask"] = np.full((128, 1), 0.0 if s == 0 else 1.0, dtype=np.float32)
        in_maps.append(m)
    return in_maps


def kernel(**inputs):
    in_maps = prepare(inputs)
    nc, _ = build_nc()
    res = run_bass_kernel_spmd(nc, in_maps, core_ids=list(range(NCORES)))
    out = np.empty((2, SEQ, DM), dtype=np.float32)
    for c in range(NCORES):
        b, s = c // 4, c % 4
        out[b, 1024 * s:1024 * s + OWN, :] = np.ascontiguousarray(res.results[c]["outT"]).T
    return out


# version 04 (final)
# speedup vs baseline: 1.0452x; 1.0452x over previous
import contextlib
import math
import numpy as np
import concourse.bass as bass
import concourse.mybir as mybir
from concourse.bass_utils import run_bass_kernel_spmd

F32 = mybir.dt.float32
AF = mybir.ActivationFunctionType
ALU = mybir.AluOpType

NCORES = 8
SEQ = 4096
DM = 2048
NT = DM // 128
HALO = 256
OWN = 1024
T = HALO + OWN
CH = [(0, 512), (512, 512), (1024, 256)]
OCH = [(256, 512), (768, 512)]
LCH = [(0, 512), (512, 512)]
HEAD_DIM = 128
ROPE_THETA = 10000.0
LN_EPS = 1e-5
RMS_EPS = 1e-6
INTER0 = 8192
NI0 = INTER0 // 128
INTER1 = 5632
NI1 = INTER1 // 128
NGCH = T // 64
NHCH = HALO // 64
NTT = T // 128

ENGN = {"T": "tensor", "V": "vector", "S": "scalar", "DL": "sync", "DG": "gpsimd"}
NDMASEM = 8
SEMKEYS = (["T", "V", "S"]
           + [f"DL{i}" for i in range(NDMASEM)] + [f"DG{i}" for i in range(NDMASEM)])


class Prog:

    def __init__(self, nc, block, sems):
        self.nc = nc
        self.block = block
        self.sems = sems
        self.cnt = {k: 0 for k in SEMKEYS}
        self.pend = {k: [] for k in ENGN}
        self.lastw = {}
        self.readers = {}
        self.wm = {k: {} for k in ENGN}
        self.rr = {"DL": 0, "DG": 0}
        self.nops = 0

    def _edge(self, waits, eng, prod):
        peng, pcnt = prod
        if peng == eng and pcnt > self.cnt[eng]:
            return
        if self.wm[eng].get(peng, 0) >= pcnt:
            return
        if waits.get(peng, 0) < pcnt:
            waits[peng] = pcnt

    def op(self, eng, fn, reads=(), writes=(), inc=True):
        waits = {}
        rn = [a.tensor.name for a in reads]
        wn = [a.tensor.name for a in writes]
        for t in rn:
            pw = self.lastw.get(t)
            if pw is not None:
                self._edge(waits, eng, pw)
        for t in wn:
            pw = self.lastw.get(t)
            if pw is not None:
                self._edge(waits, eng, pw)
            for pe, pc in self.readers.get(t, {}).items():
                self._edge(waits, eng, (pe, pc))
        if eng in ("DL", "DG"):
            sk_own = f"{eng}{self.rr[eng] % NDMASEM}"
            self.rr[eng] += 1
            step = 16
            self._edge(waits, eng, (sk_own, self.cnt[sk_own]))
            self.cnt[sk_own] += 16
            comp = self.cnt[sk_own]
            skey = sk_own
        else:
            step = 1
            skey = eng
            if inc:
                self.cnt[eng] += 1
                comp = self.cnt[eng]
            else:
                comp = self.cnt[eng] + 1
        sems = self.sems

        def run(e, fn=fn, waits=waits, inc=inc, step=step, skey=skey):
            for sk, v in waits.items():
                e.wait_ge(sems[sk], v)
            inst = fn(e)
            if inc:
                inst.then_inc(sems[skey], step)

        self.pend[eng].append(run)
        self.nops += 1
        for sk, v in waits.items():
            if self.wm[eng].get(sk, 0) < v:
                self.wm[eng][sk] = v
        for t in rn:
            dd = self.readers.setdefault(t, {})
            if dd.get(skey, 0) < comp:
                dd[skey] = comp
        for t in wn:
            self.lastw[t] = (skey, comp)
            self.readers[t] = {}

    def flush(self):
        for k, fns in self.pend.items():
            if not fns:
                continue
            self.pend[k] = []
            getattr(self.block, ENGN[k])(lambda e, fns=fns: [f(e) for f in fns])

    def barrier(self):
        self.flush()
        sems, cnt = self.sems, dict(self.cnt)
        for k in ENGN:
            waits = [(sk, cnt[sk]) for sk in SEMKEYS if sk != k and cnt[sk] > 0]

            def run(e, waits=waits):
                for sk, v in waits:
                    e.wait_ge(sems[sk], v)
            self.pend[k].append(run)
        self.flush()
        for k in ENGN:
            for sk in SEMKEYS:
                self.wm[k][sk] = cnt[sk]
        self.lastw = {}
        self.readers = {}

    def dma(self, q, out, in_):
        self.op(q, lambda e: e.dma_start(out=out, in_=in_), reads=[in_], writes=[out])

    def mm(self, ps, lhsT, rhs, start, stop):
        self.op("T", lambda e: e.matmul(ps, lhsT=lhsT, rhs=rhs, start=start, stop=stop),
                reads=[lhsT, rhs], writes=[ps])

    def tr(self, ps, in_, ident):
        self.op("T", lambda e: e.transpose(ps, in_, ident), reads=[in_, ident], writes=[ps])

    def tt(self, out, in0, in1, alu):
        self.op("V", lambda e: e.tensor_tensor(out=out, in0=in0, in1=in1, op=alu),
                reads=[in0, in1], writes=[out])

    def ts(self, out, in0, s1, op0, s2=None, op1=ALU.bypass):
        r = [in0] + [x for x in (s1, s2) if isinstance(x, bass.AP)]
        self.op("V", lambda e: e.tensor_scalar(out=out, in0=in0, scalar1=s1, scalar2=s2,
                                               op0=op0, op1=op1), reads=r, writes=[out])

    def stt(self, out, in0, s, op0, in1, op1):
        r = [in0, in1] + ([s] if isinstance(s, bass.AP) else [])
        self.op("V", lambda e: e.scalar_tensor_tensor(out=out, in0=in0, scalar=s, in1=in1,
                                                      op0=op0, op1=op1), reads=r, writes=[out])

    def act(self, out, in_, func, bias=0.0, scale=1.0):
        r = [in_] + [x for x in (bias, scale) if isinstance(x, bass.AP)]
        self.op("S", lambda e: e.activation(out=out, in_=in_, func=func, bias=bias, scale=scale),
                reads=r, writes=[out])

    def scopy(self, out, in_):
        self.op("S", lambda e: e.copy(out=out, in_=in_), reads=[in_], writes=[out])

    def vcopy(self, out, in_):
        self.op("V", lambda e: e.tensor_copy(out=out, in_=in_), reads=[in_], writes=[out])

    def memset(self, ap, v):
        self.op("V", lambda e: e.memset(ap, v), reads=[], writes=[ap])

    def recip(self, out, in_):
        self.op("V", lambda e: e.reciprocal(out=out, in_=in_), reads=[in_], writes=[out])



def build_nc():
    nc = bass.Bass("TRN2", target_bir_lowering=False)
    d = {}

    def din(name, shape, dtype=F32):
        d[name] = nc.dram_tensor(name, list(shape), dtype, kind="ExternalInput").ap()

    def dout(name, shape, dtype=F32):
        d[name] = nc.dram_tensor(name, list(shape), dtype, kind="ExternalOutput").ap()

    def dtmp(name, shape, dtype=F32):
        d[name] = nc.dram_tensor(name, list(shape), dtype, kind="Internal").ap()

    din("xbT", (DM, SEQ))
    din("xoT", (DM, T))
    din("Wq0", (DM, DM))
    din("Wk0", (DM, 512))
    din("Wv0", (DM, 512))
    din("Wo0", (DM, DM))
    din("su_ug", (128, NT))
    din("su_dp", (128, NI0))
    din("sc_up", (128, NI0))
    din("sc_gate", (128, NI0))
    din("sc_dp", (128, NT))
    din("WupT", (DM, INTER0))
    din("WgT", (DM, INTER0))
    din("WdnT", (INTER0, DM))
    din("Wq1T", (DM, DM))
    din("Wk1T", (DM, DM))
    din("Wv1T", (DM, DM))
    din("Wo1T", (DM, DM))
    din("nw1", (128, NT))
    din("G1T", (DM, INTER1))
    din("U1T", (DM, INTER1))
    din("D1T", (INTER1, DM))
    din("g1s", (128, NI1))
    din("u1s", (128, NI1))
    din("d1s", (128, NT))
    din("ropeCq", (128, T))
    din("ropeSq", (128, T))
    din("ropeCk", (128, SEQ))
    din("ropeSk", (128, SEQ))
    din("s128c", (128, 1))
    din("Pswap", (128, 128))
    din("idn", (128, 128))
    din("idn64", (128, 64))
    din("H128", (128, 128))
    din("hmask", (128, 1))
    din("LT128", (128, 128))
    din("maskT", (128, 64))

    dout("outT", (DM, OWN))

    dtmp("x1_hbm", (NT, 128, T))
    dtmp("y_hbm", (NI0, 128, T))
    dtmp("x3_hbm", (NT, 128, T))
    dtmp("x4_hbm", (NT, 128, OWN))
    dtmp("qs_hbm", (NT, 128, OWN))
    dtmp("v_hbm", (NTT, 128, DM))
    dtmp("kd_hbm", (NTT, 128, DM))
    dtmp("A_hbm", (NTT, 128, DM))

    with contextlib.ExitStack() as semstack:
        sems = {k: semstack.enter_context(nc.semaphore("sem" + k)) for k in SEMKEYS}
        block = semstack.enter_context(nc.Block())
        p = Prog(nc, block, sems)

        PS = [nc.alloc_psum_tensor(f"PS{i}", [128, 512], F32).ap() for i in range(8)]

        ones_c = nc.alloc_sbuf_tensor("ones_c", [128, 1], F32).ap()
        ones_r = nc.alloc_sbuf_tensor("ones_r", [1, 128], F32).ap()
        idn_s = nc.alloc_sbuf_tensor("idn_s", [128, 128], F32).ap()
        idn64_s = nc.alloc_sbuf_tensor("idn64_s", [128, 64], F32).ap()
        psw_s = nc.alloc_sbuf_tensor("psw_s", [128, 128], F32).ap()
        h128_s = nc.alloc_sbuf_tensor("h128_s", [128, 128], F32).ap()
        epsln = nc.alloc_sbuf_tensor("epsln", [128, 1], F32).ap()
        epsrm = nc.alloc_sbuf_tensor("epsrm", [128, 1], F32).ap()
        rs2048 = nc.alloc_sbuf_tensor("rs2048", [128, 1], F32).ap()
        rs8192 = nc.alloc_sbuf_tensor("rs8192", [128, 1], F32).ap()
        p.memset(ones_c, 1.0)
        p.memset(ones_r, 1.0)
        p.memset(epsln, LN_EPS)
        p.memset(epsrm, RMS_EPS)
        for _ap, _n in ((rs2048, 2048.0), (rs8192, 8192.0)):
            p.memset(_ap, _n)
            p.act(_ap, _ap, AF.Sqrt)
            p.recip(_ap, _ap)
        p.dma("DL", idn_s, d["idn"])
        p.dma("DL", idn64_s, d["idn64"])
        p.dma("DL", psw_s, d["Pswap"])
        p.dma("DL", h128_s, d["H128"])

        _uid = [0]

        def alloc(stack, name, shape, dtype=F32):
            _uid[0] += 1
            return stack.enter_context(
                nc.sbuf_tensor(f"{name}_u{_uid[0]}", list(shape), dtype)).ap()

        def rope(dst, W, ps_tr, Ct, St, scr):
            o = 0
            while o < W:
                w = min(512, W - o)
                p.mm(ps_tr[:, 0:w], psw_s, dst[:, o:o + w], start=True, stop=True)
                p.tt(scr[:, 0:w], ps_tr[:, 0:w], St[:, o:o + w], ALU.mult)
                p.tt(dst[:, o:o + w], dst[:, o:o + w], Ct[:, o:o + w], ALU.mult)
                p.tt(dst[:, o:o + w], dst[:, o:o + w], scr[:, 0:w], ALU.add)
                o += w

        def layernorm_tiles(xt, stack, chunks, store=None):
            ps_a, ps_b, ps_c = PS[0], PS[1], PS[2]
            sq = alloc(stack, "ln_sq", (128, 512))
            mrow = alloc(stack, "ln_m", (1, 512))
            srow = alloc(stack, "ln_s", (1, 512))
            vrow = alloc(stack, "ln_v", (1, 512))
            rrow = alloc(stack, "ln_r", (1, 512))
            mB = alloc(stack, "ln_mB", (128, 512))
            rB = alloc(stack, "ln_rB", (128, 512))
            for c0, cw in chunks:
                for j in range(NT):
                    xs = xt[j][:, c0:c0 + cw]
                    p.mm(ps_a[0:1, 0:cw], ones_c, xs, start=(j == 0), stop=(j == NT - 1))
                    p.act(sq[:, 0:cw], xs, AF.Square)
                    p.mm(ps_b[0:1, 0:cw], ones_c, sq[:, 0:cw], start=(j == 0), stop=(j == NT - 1))
                p.ts(mrow[:, 0:cw], ps_a[0:1, 0:cw], 1.0 / DM, ALU.mult)
                p.ts(srow[:, 0:cw], ps_b[0:1, 0:cw], 1.0 / DM, ALU.mult)
                p.tt(vrow[:, 0:cw], mrow[:, 0:cw], mrow[:, 0:cw], ALU.mult)
                p.tt(vrow[:, 0:cw], srow[:, 0:cw], vrow[:, 0:cw], ALU.subtract)
                p.act(vrow[:, 0:cw], vrow[:, 0:cw], AF.Sqrt, bias=epsln[0:1, :])
                p.recip(rrow[:, 0:cw], vrow[:, 0:cw])
                p.mm(ps_c[:, 0:cw], ones_r, mrow[0:1, 0:cw], start=True, stop=True)
                p.scopy(mB[:, 0:cw], ps_c[:, 0:cw])
                p.mm(ps_c[:, 0:cw], ones_r, rrow[0:1, 0:cw], start=True, stop=True)
                p.scopy(rB[:, 0:cw], ps_c[:, 0:cw])
                for j in range(NT):
                    xs = xt[j][:, c0:c0 + cw]
                    p.tt(xs, xs, mB[:, 0:cw], ALU.subtract)
                    p.tt(xs, xs, rB[:, 0:cw], ALU.mult)
                    if store is not None:
                        p.dma("DG", store(j, c0, cw), xs)

        with contextlib.ExitStack() as stP1:
            oac = [alloc(stP1, f"oac{i}", (128, T)) for i in range(NT)]
            with contextlib.ExitStack() as st1a:
                Kg = alloc(st1a, "Kg", (128, SEQ))
                Vg = alloc(st1a, "Vg", (128, SEQ))
                qf = alloc(st1a, "qf", (128, T))
                Cq = alloc(st1a, "Cq", (128, T))
                Sq = alloc(st1a, "Sq", (128, T))
                ckb = alloc(st1a, "ckb", (128, 512))
                skb = alloc(st1a, "skb", (128, 512))
                s128c = alloc(st1a, "s128c", (128, 1))
                wk_t = [alloc(st1a, f"wk{m}", (128, 128)) for m in range(NT)]
                wv_t = [alloc(st1a, f"wv{m}", (128, 128)) for m in range(NT)]
                wq_t2 = [[alloc(st1a, f"wqt{s2}_{m}", (128, 128)) for m in range(NT)]
                         for s2 in range(2)]
                xb_b = [alloc(st1a, f"xb{i}", (128, 512)) for i in range(4)]
                vts = alloc(st1a, "vts", (128, 512))
                rsc = alloc(st1a, "rsc", (128, 512))
                pb = [alloc(st1a, f"pb{i}", (128, 512)) for i in range(2)]
                Dd = alloc(st1a, "Dd", (128, 512))
                bcs = alloc(st1a, "bcs", (128, 512))
                rr1 = alloc(st1a, "rr1", (1, 512))

                p.dma("DL", Cq, d["ropeCq"])
                p.dma("DL", Sq, d["ropeSq"])
                p.dma("DL", s128c, d["s128c"])

                ps_k, ps_v, ps_tr, ps_q, ps_s0, ps_s1, ps_o, ps_sm = PS

                for g in range(4):
                    for m in range(NT):
                        p.dma("DL", wk_t[m], d["Wk0"][128 * m:128 * m + 128, 128 * g:128 * g + 128])
                        p.dma("DL", wv_t[m], d["Wv0"][128 * m:128 * m + 128, 128 * g:128 * g + 128])
                    for c in range(SEQ // 512):
                        c0 = 512 * c
                        for m in range(NT):
                            xb = xb_b[m % 4]
                            p.dma("DL", xb, d["xbT"][128 * m:128 * m + 128, c0:c0 + 512])
                            p.mm(ps_k, wk_t[m], xb, start=(m == 0), stop=(m == NT - 1))
                            p.mm(ps_v, wv_t[m], xb, start=(m == 0), stop=(m == NT - 1))
                        kf = Kg[:, c0:c0 + 512]
                        p.scopy(kf, ps_k)
                        p.dma("DL", ckb, d["ropeCk"][:, c0:c0 + 512])
                        p.dma("DL", skb, d["ropeSk"][:, c0:c0 + 512])
                        rope(kf, 512, ps_tr, ckb, skb, rsc)
                        p.scopy(vts, ps_v)
                        for kk in range(4):
                            tt_i = 4 * c + kk
                            p.tr(ps_tr[:, 0:128], vts[:, 128 * kk:128 * kk + 128], idn_s)
                            p.scopy(Vg[:, 128 * tt_i:128 * tt_i + 128], ps_tr[:, 0:128])
                    for h in range(4):
                        gh = 4 * g + h
                        wq_t = wq_t2[gh % 2]
                        for m in range(NT):
                            p.dma("DL", wq_t[m], d["Wq0"][128 * m:128 * m + 128, 128 * gh:128 * gh + 128])
                        for c0, cw in CH:
                            for m in range(NT):
                                xb = xb_b[m % 4]
                                p.dma("DL", xb[:, 0:cw], d["xoT"][128 * m:128 * m + 128, c0:c0 + cw])
                                p.mm(ps_q[:, 0:cw], wq_t[m], xb[:, 0:cw],
                                     start=(m == 0), stop=(m == NT - 1))
                            p.scopy(qf[:, c0:c0 + cw], ps_q[:, 0:cw])
                        rope(qf, T, ps_tr, Cq, Sq, rsc)
                        for qb0, qbw in CH:
                            qv = qf[:, qb0:qb0 + qbw]
                            sb = [ps_s0, ps_s1]
                            nkt = SEQ // 128
                            p.mm(sb[0][:, 0:qbw], Kg[:, 0:128], qv, start=True, stop=True)
                            for kt in range(nkt):
                                pbuf = pb[kt % 2][:, 0:qbw]
                                p.act(pbuf, sb[kt % 2][:, 0:qbw], AF.Exp, scale=s128c)
                                if kt + 1 < nkt:
                                    p.mm(sb[(kt + 1) % 2][:, 0:qbw],
                                         Kg[:, 128 * (kt + 1):128 * (kt + 2)], qv,
                                         start=True, stop=True)
                                if kt == 0:
                                    p.vcopy(Dd[:, 0:qbw], pbuf)
                                else:
                                    p.tt(Dd[:, 0:qbw], Dd[:, 0:qbw], pbuf, ALU.add)
                                p.mm(ps_o[:, 0:qbw], Vg[:, 128 * kt:128 * kt + 128], pbuf,
                                     start=(kt == 0), stop=(kt == nkt - 1))
                            p.mm(ps_sm[0:1, 0:qbw], ones_c, Dd[:, 0:qbw], start=True, stop=True)
                            p.recip(rr1[:, 0:qbw], ps_sm[0:1, 0:qbw])
                            p.mm(ps_sm[:, 0:qbw], ones_r, rr1[0:1, 0:qbw], start=True, stop=True)
                            p.scopy(bcs[:, 0:qbw], ps_sm[:, 0:qbw])
                            p.tt(oac[gh][:, qb0:qb0 + qbw], ps_o[:, 0:qbw], bcs[:, 0:qbw], ALU.mult)
                p.barrier()

            with contextlib.ExitStack() as st1b:
                x1t = [alloc(st1b, f"x1t{j}", (128, T)) for j in range(NT)]
                wo_t2 = [[alloc(st1b, f"wo{s2}_{i}", (128, 128)) for i in range(NT)]
                         for s2 in range(2)]
                xr_b = [alloc(st1b, f"xr{i}", (128, 512)) for i in range(2)]
                ps_x = PS[3]
                for j in range(NT):
                    wo_t = wo_t2[j % 2]
                    for i in range(NT):
                        p.dma("DL", wo_t[i], d["Wo0"][128 * i:128 * i + 128, 128 * j:128 * j + 128])
                    for ci, (c0, cw) in enumerate(CH):
                        for i in range(NT):
                            p.mm(ps_x[:, 0:cw], wo_t[i], oac[i][:, c0:c0 + cw],
                                 start=(i == 0), stop=(i == NT - 1))
                        xr = xr_b[ci % 2]
                        p.dma("DL", xr[:, 0:cw], d["xoT"][128 * j:128 * j + 128, c0:c0 + cw])
                        p.tt(x1t[j][:, c0:c0 + cw], ps_x[:, 0:cw], xr[:, 0:cw], ALU.add)
                layernorm_tiles(x1t, st1b, CH,
                                store=lambda j, c0, cw: d["x1_hbm"][j, :, c0:c0 + cw])
                p.barrier()

        with contextlib.ExitStack() as stP2:
            with contextlib.ExitStack() as st2a:
                zA = [alloc(st2a, f"zA{j}", (128, T)) for j in range(NT)]
                with contextlib.ExitStack() as st2z:
                    zB = [alloc(st2z, f"zB{j}", (128, T)) for j in range(NT)]
                    Hs = [alloc(st2z, f"Hsu{j}", (128, 128)) for j in range(NT)]
                    suc = alloc(st2z, "suc", (128, NT))
                    xb_b = [alloc(st2z, f"fxb{i}", (128, 512)) for i in range(2)]
                    p.dma("DL", suc, d["su_ug"])
                    for j in range(NT):
                        p.ts(Hs[j], h128_s, suc[:, j:j + 1], ALU.mult)
                    for j in range(NT):
                        for ci, (c0, cw) in enumerate(CH):
                            xb = xb_b[ci % 2]
                            p.dma("DL", xb[:, 0:cw], d["x1_hbm"][j, :, c0:c0 + cw])
                            p.mm(PS[0][:, 0:cw], Hs[j], xb[:, 0:cw], start=True, stop=True)
                            p.scopy(zA[j][:, c0:c0 + cw], PS[0][:, 0:cw])
                    cur, nxt = zA, zB
                    for hd in (1, 2, 4, 8):
                        for j in range(NT):
                            a, b2 = cur[j & ~hd], cur[j | hd]
                            p.tt(nxt[j], a, b2, ALU.add if (j & hd) == 0 else ALU.subtract)
                        cur, nxt = nxt, cur
                    assert cur is zA
                    p.barrier()

                wu_t2 = [[alloc(st2a, f"wu{s2}_{m}", (128, 128)) for m in range(NT)]
                         for s2 in range(2)]
                wg_t2 = [[alloc(st2a, f"wg{s2}_{m}", (128, 128)) for m in range(NT)]
                         for s2 in range(2)]
                silb = alloc(st2a, "silb", (128, 512))
                upb = alloc(st2a, "upb", (128, 512))
                yb = alloc(st2a, "yb", (128, 512))
                scU = alloc(st2a, "scU", (128, NI0))
                scG = alloc(st2a, "scG", (128, NI0))
                sud = alloc(st2a, "sud", (128, NI0))
                p.dma("DL", scU, d["sc_up"])
                p.dma("DL", scG, d["sc_gate"])
                p.dma("DL", sud, d["su_dp"])
                p.ts(scU, scU, rs2048, ALU.mult)
                p.ts(scG, scG, rs2048, ALU.mult)
                ps_u, ps_g = PS[1], PS[2]
                for i in range(NI0):
                    wu_t, wg_t = wu_t2[i % 2], wg_t2[i % 2]
                    for m in range(NT):
                        p.dma("DL", wu_t[m], d["WupT"][128 * m:128 * m + 128, 128 * i:128 * i + 128])
                        p.dma("DL", wg_t[m], d["WgT"][128 * m:128 * m + 128, 128 * i:128 * i + 128])
                    for c0, cw in CH:
                        for m in range(NT):
                            p.mm(ps_u[:, 0:cw], wu_t[m], zA[m][:, c0:c0 + cw],
                                 start=(m == 0), stop=(m == NT - 1))
                            p.mm(ps_g[:, 0:cw], wg_t[m], zA[m][:, c0:c0 + cw],
                                 start=(m == 0), stop=(m == NT - 1))
                        p.act(silb[:, 0:cw], ps_g[:, 0:cw], AF.Silu, scale=scG[:, i:i + 1])
                        p.act(upb[:, 0:cw], ps_u[:, 0:cw], AF.Copy, scale=scU[:, i:i + 1])
                        p.stt(yb[:, 0:cw], silb[:, 0:cw], sud[:, i:i + 1], ALU.mult,
                              upb[:, 0:cw], ALU.mult)
                        p.dma("DG", d["y_hbm"][i, :, c0:c0 + cw], yb[:, 0:cw])
                p.barrier()

            x2t = [alloc(stP2, f"x2t{j}", (128, T)) for j in range(NT)]
            with contextlib.ExitStack() as st2c:
                yA = [alloc(st2c, f"yA{i}", (128, 128)) for i in range(NI0)]
                yB = [alloc(st2c, f"yB{i}", (128, 128)) for i in range(NI0)]
                wd_t = [alloc(st2c, f"wd{i}", (128, 128)) for i in range(16)]
                xr_b = [alloc(st2c, f"cxr{i}", (128, 128)) for i in range(2)]
                scD = alloc(st2c, "scD", (128, NT))
                p.dma("DL", scD, d["sc_dp"])
                p.ts(scD, scD, rs8192, ALU.mult)
                for c5 in range(T // 128):
                    c0 = 128 * c5
                    for i in range(NI0):
                        p.dma("DL", yA[i], d["y_hbm"][i, :, c0:c0 + 128])
                        p.mm(PS[0][:, 0:128], h128_s, yA[i], start=True, stop=True)
                        p.scopy(yA[i], PS[0][:, 0:128])
                    cur, nxt = yA, yB
                    for hd in (1, 2, 4, 8, 16, 32):
                        for i in range(NI0):
                            a, b2 = cur[i & ~hd], cur[i | hd]
                            p.tt(nxt[i], a, b2, ALU.add if (i & hd) == 0 else ALU.subtract)
                        cur, nxt = nxt, cur
                    assert cur is yA
                    for jg in range(2):
                        for i in range(NI0):
                            for j8 in range(8):
                                j = 8 * jg + j8
                                wd = wd_t[8 * (i % 2) + j8]
                                p.dma("DL", wd,
                                      d["WdnT"][128 * i:128 * i + 128, 128 * j:128 * j + 128])
                                p.mm(PS[j8][:, 0:128], wd, yA[i],
                                     start=(i == 0), stop=(i == NI0 - 1))
                        for j8 in range(8):
                            j = 8 * jg + j8
                            xr = xr_b[j8 % 2]
                            p.dma("DL", xr, d["x1_hbm"][j, :, c0:c0 + 128])
                            p.stt(x2t[j][:, c0:c0 + 128], PS[j8][:, 0:128], scD[:, j:j + 1],
                                  ALU.mult, xr, ALU.add)
                p.barrier()

            with contextlib.ExitStack() as st2d:
                layernorm_tiles(x2t, st2d, CH,
                                store=lambda j, c0, cw: d["x3_hbm"][j, :, c0:c0 + cw])
                p.barrier()

            x3t = x2t
            with contextlib.ExitStack() as st3a:
                wq12 = [[alloc(st3a, f"wq1{s2}_{m}", (128, 128)) for m in range(NT)]
                        for s2 in range(2)]
                qsb = alloc(st3a, "qsb", (128, 512))
                ps_q = PS[3]
                for et in range(NT):
                    wq1 = wq12[et % 2]
                    for m in range(NT):
                        p.dma("DL", wq1[m], d["Wq1T"][128 * m:128 * m + 128, 128 * et:128 * et + 128])
                    for c0, cw in OCH:
                        for m in range(NT):
                            p.mm(ps_q[:, 0:cw], wq1[m], x3t[m][:, c0:c0 + cw],
                                 start=(m == 0), stop=(m == NT - 1))
                        p.act(qsb[:, 0:cw], ps_q[:, 0:cw], AF.Silu)
                        p.dma("DG", d["qs_hbm"][et, :, c0 - HALO:c0 - HALO + cw], qsb[:, 0:cw])

                wkc = [alloc(st3a, f"wkc{i}", (128, 512)) for i in range(2)]
                wvc = [alloc(st3a, f"wvc{i}", (128, 512)) for i in range(2)]
                fb = alloc(st3a, "fb", (128, 512))
                kb = alloc(st3a, "kb", (128, 512))
                eb = alloc(st3a, "eb", (128, 512))
                Ab = alloc(st3a, "Ab", (128, 512))
                vb = alloc(st3a, "vb", (128, 512))
                lt128 = alloc(st3a, "lt128", (128, 128))
                hmk = alloc(st3a, "hmk", (128, 1))
                p.dma("DL", lt128, d["LT128"])
                p.dma("DL", hmk, d["hmask"])
                ps_f, ps_v, ps_A = PS[4], PS[5], PS[6]
                for tt_i in range(NTT):
                    for ec in range(4):
                        e0 = 512 * ec
                        for m in range(NT):
                            w1, w2 = wkc[m % 2], wvc[m % 2]
                            p.dma("DL", w1, d["Wk1T"][128 * m:128 * m + 128, e0:e0 + 512])
                            p.dma("DL", w2, d["Wv1T"][128 * m:128 * m + 128, e0:e0 + 512])
                            xsl = x3t[m][:, 128 * tt_i:128 * tt_i + 128]
                            p.mm(ps_f, xsl, w1, start=(m == 0), stop=(m == NT - 1))
                            p.mm(ps_v, xsl, w2, start=(m == 0), stop=(m == NT - 1))
                        p.act(fb, ps_f, AF.Sigmoid)
                        p.act(kb, fb, AF.Sigmoid, scale=-1.0)
                        p.act(fb, fb, AF.Sigmoid)
                        p.act(fb, fb, AF.Ln)
                        p.mm(ps_A, lt128, fb, start=True, stop=True)
                        p.scopy(Ab, ps_A)
                        p.act(eb, ps_A, AF.Exp, scale=-1.0)
                        p.tt(kb, kb, eb, ALU.mult)
                        if tt_i < 2:
                            p.act(vb, ps_v, AF.Copy, scale=hmk)
                        else:
                            p.scopy(vb, ps_v)
                        p.dma("DG", d["A_hbm"][tt_i, :, e0:e0 + 512], Ab)
                        p.dma("DG", d["kd_hbm"][tt_i, :, e0:e0 + 512], kb)
                        p.dma("DG", d["v_hbm"][tt_i, :, e0:e0 + 512], vb)
                p.barrier()

        with contextlib.ExitStack() as stP3:
            ot = [alloc(stP3, f"ot{i}", (128, OWN)) for i in range(NT)]
            with contextlib.ExitStack() as st3b:
                qsg = [alloc(st3b, f"qsg{h}", (128, OWN)) for h in range(4)]
                vg = [alloc(st3b, f"vg{c}", (128, 512)) for c in range(NTT)]
                kdg = [alloc(st3b, f"kdg{c}", (128, 512)) for c in range(NTT)]
                Ag = [alloc(st3b, f"Agc{c}", (128, 512)) for c in range(NTT)]
                kdT = [alloc(st3b, f"kdT{h}", (128, OWN)) for h in range(4)]
                AfT = alloc(st3b, "AfT", (128, T))
                dec = [alloc(st3b, f"dec{h}", (128, NGCH)) for h in range(4)]
                Ssb = [alloc(st3b, f"Ssb{h}", (128, 128)) for h in range(4)]
                psb = alloc(st3b, "psb", (128, 64))
                mskT = alloc(st3b, "mskT", (128, 64))
                p.dma("DL", mskT, d["maskT"])
                ps_tr, ps_sc, ps_go, ps_sd = PS[0], PS[1], PS[2], PS[3]
                for g in range(4):
                    e0 = 512 * g
                    for h in range(4):
                        p.dma("DL", qsg[h], d["qs_hbm"][4 * g + h])
                    for c in range(NTT):
                        p.dma("DL", vg[c], d["v_hbm"][c, :, e0:e0 + 512])
                        p.dma("DL", kdg[c], d["kd_hbm"][c, :, e0:e0 + 512])
                        p.dma("DL", Ag[c], d["A_hbm"][c, :, e0:e0 + 512])
                    for h in range(4):
                        he = 128 * h
                        for c in range(NGCH):
                            r0 = 64 * (c % 2)
                            tle = Ag[c // 2][r0:r0 + 64, he:he + 128]
                            p.tr(ps_tr[:, 0:64], tle, idn64_s[r0:r0 + 64, :])
                            p.scopy(AfT[:, 64 * c:64 * c + 64], ps_tr[:, 0:64])
                            if c >= NHCH:
                                oc = c - NHCH
                                tle = kdg[c // 2][r0:r0 + 64, he:he + 128]
                                p.tr(ps_tr[:, 64:128], tle, idn64_s[r0:r0 + 64, :])
                                p.scopy(kdT[h][:, 64 * oc:64 * oc + 64], ps_tr[:, 64:128])
                        p.act(AfT, AfT, AF.Exp)
                        p.ts(dec[h], AfT[:, 63::64], 1.0, ALU.mult)
                        p.tt(qsg[h], qsg[h], AfT[:, HALO:], ALU.mult)
                    for c in range(NGCH):
                        r0 = 64 * (c % 2)
                        for h in range(4):
                            he = 128 * h
                            gh = 4 * g + h
                            if c >= NHCH:
                                oc = c - NHCH
                                t0 = 64 * oc
                                p.mm(ps_sc[r0:r0 + 64, 0:64], kdT[h][:, t0:t0 + 64],
                                     qsg[h][:, t0:t0 + 64], start=True, stop=True)
                                p.tt(psb[r0:r0 + 64, :], ps_sc[r0:r0 + 64, 0:64],
                                     mskT[r0:r0 + 64, :], ALU.mult)
                                p.mm(ps_go[:, 0:64], vg[c // 2][r0:r0 + 64, he:he + 128],
                                     psb[r0:r0 + 64, :], start=True, stop=False)
                                p.mm(ps_go[:, 0:64], Ssb[h], qsg[h][:, t0:t0 + 64],
                                     start=False, stop=True)
                                p.scopy(ot[gh][:, t0:t0 + 64], ps_go[:, 0:64])
                            p.mm(ps_sd[:, 0:128], kdg[c // 2][r0:r0 + 64, he:he + 128],
                                 vg[c // 2][r0:r0 + 64, he:he + 128], start=True, stop=True)
                            if c == 0:
                                p.ts(Ssb[h], ps_sd[:, 0:128], dec[h][:, 0:1], ALU.mult)
                            else:
                                p.tt(Ssb[h], Ssb[h], ps_sd[:, 0:128], ALU.add)
                                p.ts(Ssb[h], Ssb[h], dec[h][:, c:c + 1], ALU.mult)
                p.barrier()

            with contextlib.ExitStack() as st3c:
                x4t = [alloc(st3c, f"x4t{j}", (128, OWN)) for j in range(NT)]
                nwc = alloc(st3c, "nwc", (128, NT))
                sq = alloc(st3c, "rm_sq", (128, 512))
                srow = alloc(st3c, "rm_s", (1, 512))
                rrow = alloc(st3c, "rm_r", (1, 512))
                rB = alloc(st3c, "rm_rB", (128, 512))
                p.dma("DL", nwc, d["nw1"])
                for c0, cw in LCH:
                    for j in range(NT):
                        p.act(sq[:, 0:cw], ot[j][:, c0:c0 + cw], AF.Square)
                        p.mm(PS[0][0:1, 0:cw], ones_c, sq[:, 0:cw],
                             start=(j == 0), stop=(j == NT - 1))
                    p.ts(srow[:, 0:cw], PS[0][0:1, 0:cw], 1.0 / DM, ALU.mult)
                    p.act(srow[:, 0:cw], srow[:, 0:cw], AF.Sqrt, bias=epsrm[0:1, :])
                    p.recip(rrow[:, 0:cw], srow[:, 0:cw])
                    p.mm(PS[1][:, 0:cw], ones_r, rrow[0:1, 0:cw], start=True, stop=True)
                    p.scopy(rB[:, 0:cw], PS[1][:, 0:cw])
                    for j in range(NT):
                        p.stt(ot[j][:, c0:c0 + cw], ot[j][:, c0:c0 + cw], nwc[:, j:j + 1],
                              ALU.mult, rB[:, 0:cw], ALU.mult)
                wo12 = [[alloc(st3c, f"wo1{s2}_{i}", (128, 128)) for i in range(NT)]
                        for s2 in range(2)]
                xr_b = [alloc(st3c, f"xr3{i}", (128, 512)) for i in range(2)]
                for j in range(NT):
                    wo1 = wo12[j % 2]
                    for i in range(NT):
                        p.dma("DL", wo1[i], d["Wo1T"][128 * i:128 * i + 128, 128 * j:128 * j + 128])
                    for ci, (c0, cw) in enumerate(LCH):
                        for i in range(NT):
                            p.mm(PS[2][:, 0:cw], wo1[i], ot[i][:, c0:c0 + cw],
                                 start=(i == 0), stop=(i == NT - 1))
                        xr = xr_b[ci % 2]
                        p.dma("DL", xr[:, 0:cw], d["x3_hbm"][j, :, HALO + c0:HALO + c0 + cw])
                        p.tt(x4t[j][:, c0:c0 + cw], PS[2][:, 0:cw], xr[:, 0:cw], ALU.add)
                layernorm_tiles(x4t, st3c, LCH,
                                store=lambda j, c0, cw: d["x4_hbm"][j, :, c0:c0 + cw])
                p.barrier()

        with contextlib.ExitStack() as stP4:
            x5t = [alloc(stP4, f"x5t{j}", (128, OWN)) for j in range(NT)]
            with contextlib.ExitStack() as st4a:
                xc = [alloc(st4a, f"xc{m}", (128, 256)) for m in range(NT)]
                y1 = [alloc(st4a, f"y1{i}", (128, 256)) for i in range(NI1)]
                wgt2 = [[alloc(st4a, f"wgt{s2}_{m}", (128, 128)) for m in range(NT)]
                        for s2 in range(2)]
                wut2 = [[alloc(st4a, f"wut{s2}_{m}", (128, 128)) for m in range(NT)]
                        for s2 in range(2)]
                wdt = [alloc(st4a, f"wdt{i}", (128, 128)) for i in range(16)]
                silb = alloc(st4a, "sil4", (128, 256))
                upb = alloc(st4a, "up4", (128, 256))
                g1c = alloc(st4a, "g1c", (128, NI1))
                u1c = alloc(st4a, "u1c", (128, NI1))
                d1c = alloc(st4a, "d1c", (128, NT))
                xr_b = [alloc(st4a, f"xr4{i}", (128, 256)) for i in range(2)]
                p.dma("DL", g1c, d["g1s"])
                p.dma("DL", u1c, d["u1s"])
                p.dma("DL", d1c, d["d1s"])
                ps_g, ps_u = PS[6], PS[7]
                for c4 in range(OWN // 256):
                    c0 = 256 * c4
                    for m in range(NT):
                        p.dma("DL", xc[m], d["x4_hbm"][m, :, c0:c0 + 256])
                    for i in range(NI1):
                        wgt, wut = wgt2[i % 2], wut2[i % 2]
                        for m in range(NT):
                            p.dma("DL", wgt[m], d["G1T"][128 * m:128 * m + 128, 128 * i:128 * i + 128])
                            p.dma("DL", wut[m], d["U1T"][128 * m:128 * m + 128, 128 * i:128 * i + 128])
                            p.mm(ps_g[:, 0:256], wgt[m], xc[m], start=(m == 0), stop=(m == NT - 1))
                            p.mm(ps_u[:, 0:256], wut[m], xc[m], start=(m == 0), stop=(m == NT - 1))
                        p.act(silb, ps_g[:, 0:256], AF.Silu, scale=g1c[:, i:i + 1])
                        p.act(upb, ps_u[:, 0:256], AF.Copy, scale=u1c[:, i:i + 1])
                        p.tt(y1[i], silb, upb, ALU.mult)
                    for jg in range(2):
                        for i in range(NI1):
                            for j8 in range(8):
                                j = 8 * jg + j8
                                wd = wdt[8 * (i % 2) + j8]
                                p.dma("DL", wd, d["D1T"][128 * i:128 * i + 128, 128 * j:128 * j + 128])
                                p.mm(PS[j8][:, 0:256], wd, y1[i],
                                     start=(i == 0), stop=(i == NI1 - 1))
                        for j8 in range(8):
                            j = 8 * jg + j8
                            xr = xr_b[j8 % 2]
                            p.dma("DL", xr, d["x4_hbm"][j, :, c0:c0 + 256])
                            p.stt(x5t[j][:, c0:c0 + 256], PS[j8][:, 0:256], d1c[:, j:j + 1],
                                  ALU.mult, xr, ALU.add)
                p.barrier()
            with contextlib.ExitStack() as st4b:
                layernorm_tiles(x5t, st4b, LCH,
                                store=lambda j, c0, cw: d["outT"][128 * j:128 * j + 128, c0:c0 + cw])
                p.barrier()

        p.flush()
    return nc, d



def _rope_trig(positions):
    pos = np.asarray(positions)
    try:
        import jax
        import jax.numpy as jnp
        with jax.default_device(jax.devices("cpu")[0]):
            inv = 1.0 / ROPE_THETA ** (jnp.arange(0, 64, 2, dtype=jnp.float32) / 64)
            fr = pos.astype(np.float32)[None, :] * inv[:, None]
            cos = np.asarray(jnp.cos(fr), np.float32)
            sin = np.asarray(jnp.sin(fr), np.float32)
            return cos, sin
    except Exception:
        e = np.arange(0, 64, 2, dtype=np.float32) / np.float32(64)
        inv = (np.float32(1.0) / (np.float32(ROPE_THETA) ** e)).astype(np.float32)
        fr = (pos.astype(np.float32)[None, :] * inv[:, None]).astype(np.float32)
        return np.cos(fr).astype(np.float32), np.sin(fr).astype(np.float32)


def _rope_tables(positions):
    n = len(positions)
    cos, sin = _rope_trig(positions)
    C = np.ones((128, n), dtype=np.float32)
    S = np.zeros((128, n), dtype=np.float32)
    C[0:64:2] = cos
    C[1:64:2] = cos
    S[0:64:2] = -sin
    S[1:64:2] = sin
    return C, S


def _attn_scale():
    try:
        import jax
        import jax.numpy as jnp
        with jax.default_device(jax.devices("cpu")[0]):
            return float(1.0 / jnp.sqrt(jnp.float32(HEAD_DIM)))
    except Exception:
        return float(np.float32(1.0) / np.float32(np.sqrt(np.float32(HEAD_DIM))))


def _hadamard(n):
    H = np.array([[1.0]], dtype=np.float32)
    while H.shape[0] < n:
        H = np.block([[H, H], [H, -H]])
    return np.ascontiguousarray(H.astype(np.float32))


def _cols(v, rows=128):
    return np.ascontiguousarray(np.asarray(v, np.float32).reshape(-1, rows).T)


def prepare(inputs):
    ins = {k: np.asarray(v, dtype=np.float32) for k, v in inputs.items()}
    x = ins["L0_attn_hidden_states"]

    Pswap = np.zeros((128, 128), dtype=np.float32)
    for j in range(32):
        Pswap[2 * j, 2 * j + 1] = 1.0
        Pswap[2 * j + 1, 2 * j] = 1.0
    idn = np.eye(128, dtype=np.float32)
    idn64 = np.zeros((128, 64), dtype=np.float32)
    idn64[0:64] = np.eye(64, dtype=np.float32)
    idn64[64:128] = np.eye(64, dtype=np.float32)
    H128 = _hadamard(128)
    Ck, Sk = _rope_tables(np.arange(SEQ))
    tri = np.triu(np.ones((64, 64), dtype=np.float32))
    LT128 = np.zeros((128, 128), dtype=np.float32)
    LT128[0:64, 0:64] = tri
    LT128[64:128, 64:128] = tri
    maskT = np.concatenate([tri, tri], axis=0).astype(np.float32)

    base = {
        "Wq0": ins["L0_attn_Wq"], "Wk0": ins["L0_attn_Wk"],
        "Wv0": ins["L0_attn_Wv"], "Wo0": ins["L0_attn_Wo"],
        "su_ug": _cols(ins["L0_ffn_SU_ug"]), "su_dp": _cols(ins["L0_ffn_SU_dp"]),
        "sc_up": _cols(ins["L0_ffn_Wscale_ug"][:INTER0]),
        "sc_gate": _cols(ins["L0_ffn_Wscale_ug"][INTER0:]),
        "sc_dp": _cols(ins["L0_ffn_Wscale_dp"]),
        "WupT": np.ascontiguousarray(ins["L0_ffn_W_up"].T),
        "WgT": np.ascontiguousarray(ins["L0_ffn_W_gate"].T),
        "WdnT": np.ascontiguousarray(ins["L0_ffn_W_down"].T),
        "Wq1T": np.ascontiguousarray(ins["L1_attn_Wq"].T),
        "Wk1T": np.ascontiguousarray(ins["L1_attn_Wk"].T),
        "Wv1T": np.ascontiguousarray(ins["L1_attn_Wv"].T),
        "Wo1T": np.ascontiguousarray(ins["L1_attn_Wo"].T),
        "nw1": _cols(ins["L1_attn_norm_w"]),
        "G1T": np.ascontiguousarray(ins["L1_ffn_gate_w"].T),
        "U1T": np.ascontiguousarray(ins["L1_ffn_up_w"].T),
        "D1T": np.ascontiguousarray(ins["L1_ffn_down_w"].T),
        "g1s": _cols(ins["L1_ffn_gate_s"]), "u1s": _cols(ins["L1_ffn_up_s"]),
        "d1s": _cols(ins["L1_ffn_down_s"]),
        "ropeCk": Ck, "ropeSk": Sk,
        "s128c": np.full((128, 1), _attn_scale(), dtype=np.float32),
        "Pswap": Pswap, "idn": idn, "idn64": idn64, "H128": H128,
        "LT128": LT128, "maskT": maskT,
    }

    in_maps = []
    for c in range(NCORES):
        b, s = c // 4, c % 4
        xbT = np.ascontiguousarray(x[b].T)
        lo = 1024 * s - HALO
        if s == 0:
            halo = np.zeros((DM, HALO), dtype=np.float32)
            pos_h = np.zeros(HALO, dtype=np.int64)
        else:
            halo = xbT[:, lo:lo + HALO]
            pos_h = np.arange(lo, lo + HALO)
        own = xbT[:, 1024 * s:1024 * s + OWN]
        xoT = np.ascontiguousarray(np.concatenate([halo, own], axis=1))
        pos = np.concatenate([pos_h, np.arange(1024 * s, 1024 * s + OWN)])
        Cq, Sq = _rope_tables(pos)
        m = dict(base)
        m["xbT"] = xbT
        m["xoT"] = xoT
        m["ropeCq"] = Cq
        m["ropeSq"] = Sq
        m["hmask"] = np.full((128, 1), 0.0 if s == 0 else 1.0, dtype=np.float32)
        in_maps.append(m)
    return in_maps


def kernel(**inputs):
    in_maps = prepare(inputs)
    nc, _ = build_nc()
    res = run_bass_kernel_spmd(nc, in_maps, core_ids=list(range(NCORES)))
    out = np.empty((2, SEQ, DM), dtype=np.float32)
    for c in range(NCORES):
        b, s = c // 4, c % 4
        out[b, 1024 * s:1024 * s + OWN, :] = np.ascontiguousarray(res.results[c]["outT"]).T
    return out
